# version 05 (final)
# baseline (speedup 1.0000x reference)
import numpy as np
import ml_dtypes
import concourse.bass as bass
import concourse.mybir as mybir
from concourse.bass_utils import run_bass_kernel_spmd
from contextlib import ExitStack

F32 = mybir.dt.float32
BF16 = mybir.dt.bfloat16
AF = mybir.ActivationFunctionType
ALU = mybir.AluOpType
AX = mybir.AxisListType

T = 4096
D = 2048
DIN = 4096
NHm = 64
CONVD = 4224
DPJ = 8384
I0 = 8192
F1 = 5504
NTC = 32
RT128 = 0.08838834764831845
MAGIC = float(np.float32(1.5 * 2 ** 23))


def build():
    nc = bass.Bass("TRN2", target_bir_lowering=False, debug=False)
    ap = lambda n, s, d, k: nc.dram_tensor(n, s, d, kind=k).ap()
    X = ap("X", [T, D], F32, "ExternalInput")
    WinT = ap("WinT", [D, DPJ], F32, "ExternalInput")
    CW = ap("CW", [128, 132], F32, "ExternalInput")
    CB = ap("CB", [128, 33], F32, "ExternalInput")
    DTB = ap("DTB", [64, 1], F32, "ExternalInput")
    ALOG = ap("ALOG", [1, 64], F32, "ExternalInput")
    DROW = ap("DROW", [1, 64], F32, "ExternalInput")
    NW = ap("NW", [1, DIN], F32, "ExternalInput")
    NWC = ap("NWC", [128, 32], F32, "ExternalInput")
    WoutT = ap("WoutT", [DIN, D], F32, "ExternalInput")
    WG0 = ap("WG0", [D, I0], F32, "ExternalInput")
    WU0 = ap("WU0", [D, I0], F32, "ExternalInput")
    WD0 = ap("WD0", [I0, D], F32, "ExternalInput")
    WQT = ap("WQT", [D, D], F32, "ExternalInput")
    WKT = ap("WKT", [D, 512], F32, "ExternalInput")
    WVT = ap("WVT", [D, 512], F32, "ExternalInput")
    WPT = ap("WPT", [D, D], F32, "ExternalInput")
    QG = ap("QG", [1, 16], F32, "ExternalInput")
    WG1 = ap("WG1", [D, F1], F32, "ExternalInput")
    WU1 = ap("WU1", [D, F1], F32, "ExternalInput")
    WD1 = ap("WD1", [F1, D], F32, "ExternalInput")
    I128 = ap("I128", [128, 128], F32, "ExternalInput")
    TRI = ap("TRI", [128, 128], F32, "ExternalInput")
    TRIB = ap("TRIB", [128, 128], BF16, "ExternalInput")
    DMSK = ap("DMSK", [4, 128, 512], BF16, "ExternalInput")
    COST = ap("COST", [T, 8], F32, "ExternalInput")
    SINT = ap("SINT", [T, 8], F32, "ExternalInput")
    OUT = ap("OUT", [T, D], F32, "ExternalOutput")
    zsp = nc.dram_tensor("zsp", [T, DIN], BF16).ap()
    xbcf = nc.dram_tensor("xbcf", [CONVD, T], BF16).ap()
    dtsp = nc.dram_tensor("dtsp", [T, 64], F32).ap()
    hsp = nc.dram_tensor("hsp", [I0, T], BF16).ap()
    g0sp = nc.dram_tensor("g0sp", [T, D], BF16).ap()
    x1sp = nc.dram_tensor("x1sp", [T, D], F32).ap()
    ynsp = nc.dram_tensor("ynsp", [DIN, T], BF16).ap()
    qfsp = nc.dram_tensor("qfsp", [D, T], BF16).ap()
    xqsp = nc.dram_tensor("xqsp", [D, T], BF16).ap()
    kfsp = nc.dram_tensor("kfsp", [512, T], BF16).ap()
    vtsp = nc.dram_tensor("vtsp", [T, 512], BF16).ap()
    yfsp = nc.dram_tensor("yfsp", [D, T], BF16).ap()
    s1sp = nc.dram_tensor("s1sp", [T, D], BF16).ap()
    wq1g = nc.dram_tensor("wq1g", [D, F1], BF16).ap()
    wq1u = nc.dram_tensor("wq1u", [D, F1], BF16).ap()
    wq1d = nc.dram_tensor("wq1d", [F1, D], BF16).ap()
    h1sp = nc.dram_tensor("h1sp", [T, F1], BF16).ap()

    st = ExitStack()
    E = st.enter_context
    sb = lambda n, s, d: E(nc.sbuf_tensor(n, s, d))
    g = E(nc.semaphore("g"))
    block = E(nc.Block())
    C = {"v": 0}

    def seq(inst, amt=1):
        inst.then_inc(g, amt)
        C["v"] += amt

    def dma(eng, out, in_):
        seq(eng.dma_start(out=out, in_=in_), 16)

    def ten(fn):
        def _f(e):
            e.wait_ge(g, C["v"]); fn(e)
        block.tensor(_f)

    def vec(fn):
        def _f(e):
            e.wait_ge(g, C["v"]); fn(e)
        block.vector(_f)

    def act(fn):
        def _f(e):
            e.wait_ge(g, C["v"]); fn(e)
        block.scalar(_f)

    def dms(fn):
        def _f(e):
            e.wait_ge(g, C["v"]); fn(e)
        block.sync(_f)

    def dmg(fn):
        def _f(e):
            e.wait_ge(g, C["v"]); fn(e)
        block.gpsimd(_f)

    i128 = sb("i128", [128, 128], F32)
    tri = sb("tri", [128, 128], F32)
    trib = sb("trib", [128, 128], BF16)
    cost = sb("cost", [128, 32 * 8], F32)
    sint = sb("sint", [128, 32 * 8], F32)
    onesr = sb("onesr", [1, 128], F32)
    onescf = sb("onescf", [128, 1], F32)
    onescb = sb("onescb", [128, 1], BF16)
    cw = sb("cw", [128, 132], F32)
    cb = sb("cb", [128, 33], F32)
    dtb = sb("dtb", [128, 1], F32)
    alogr = sb("alogr", [1, 64], F32)
    drow = sb("drow", [1, 64], F32)
    nwrow = sb("nwrow", [1, DIN], F32)
    qgrow = sb("qgrow", [1, 16], F32)
    anegr = sb("anegr", [1, 64], F32)
    anegb = sb("anegb", [128, 64], F32)
    qgb = sb("qgb", [128, 16], F32)
    amax_all = sb("amax_all", [128, 32], F32)
    amaxs = sb("amaxs", [128, 32], F32)
    hmaxes = sb("hmaxes", [128, 32 * 11], F32)
    r1 = sb("r1", [128, 1], F32)
    s1c = sb("s1c", [128, 1], F32)
    s2c = sb("s2c", [128, 1], F32)
    mcol = sb("mcol", [128, 1], F32)
    qcol = sb("qcol", [128, 1], F32)
    mmc = sb("mmc", [128, 1], F32)
    vcol = sb("vcol", [128, 1], F32)

    def ln_emit(xt, xnout, pTT):
        def v1(v):
            seq(v.reduce_sum(s1c[:, :], xt[:, :], AX.X))
        vec(v1)
        def a1(a):
            seq(a.activation(xnout[:, :], xt[:, :], AF.Square, accum_out=s2c[:, :]))
        act(a1)
        def v2(v):
            seq(v.tensor_scalar(mcol[:, :], s1c[:, :], 1.0 / 2048, None, ALU.mult))
            seq(v.tensor_scalar(qcol[:, :], s2c[:, :], 1.0 / 2048, 1e-5, ALU.mult, ALU.add))
            v.drain()
            seq(v.tensor_tensor(mmc[:, :], mcol[:, :], mcol[:, :], ALU.mult))
            v.drain()
            seq(v.tensor_tensor(vcol[:, :], qcol[:, :], mmc[:, :], ALU.subtract))
        vec(v2)
        def a2(a):
            seq(a.activation(qcol[:, :], vcol[:, :], AF.Sqrt))
        act(a2)
        def v3(v):
            seq(v.reciprocal(r1[:, :], qcol[:, :]))
            v.drain()
            seq(v.tensor_scalar(xnout[:, :], xt[:, :], mcol[:, :], r1[:, :], ALU.subtract, ALU.mult))
        vec(v3)

    def transpose16(src_f32, dst_bf, pTT, n=16):
        def t1(t):
            for j in range(n):
                seq(t.transpose(pTT[:, 128 * j:128 * (j + 1)], src_f32[:, 128 * j:128 * (j + 1)], i128[:, :]))
        ten(t1)
        def v1(v):
            for j in range(n):
                seq(v.tensor_copy(dst_bf[:, 128 * j:128 * (j + 1)], pTT[:, 128 * j:128 * (j + 1)]))
        vec(v1)

    def d0(s):
        dma(s, i128[:, :], I128)
        dma(s, tri[:, :], TRI)
        dma(s, trib[:, :], TRIB)
        for tc in range(32):
            dma(s, cost[:, 8 * tc:8 * (tc + 1)], COST[128 * tc:128 * (tc + 1), :])
            dma(s, sint[:, 8 * tc:8 * (tc + 1)], SINT[128 * tc:128 * (tc + 1), :])
        dma(s, cw[:, :], CW)
        dma(s, cb[:, :], CB)
        dma(s, dtb[0:64, :], DTB)
        dma(s, alogr[:, :], ALOG)
        dma(s, drow[:, :], DROW)
        dma(s, nwrow[:, :], NW)
        dma(s, qgrow[:, :], QG)
    dms(d0)
    def v0(v):
        seq(v.memset(onesr[:, :], 1.0))
        seq(v.memset(onescf[:, :], 1.0))
        seq(v.memset(onescb[:, :], 1.0))
    vec(v0)
    def a0(a):
        seq(a.activation(anegr[:, :], alogr[:, :], AF.Exp, scale=1.0))
    act(a0)
    def v0b(v):
        seq(v.tensor_scalar(anegr[:, :], anegr[:, :], -1.0, None, ALU.mult))
    vec(v0b)
    with nc.psum_tensor("pinit", [128, 512], F32) as pinit:
        def t0(t):
            seq(t.matmul(pinit[:, 0:64], onesr[:, :], anegr[:, :], start=True, stop=True))
            seq(t.matmul(pinit[:, 64:80], onesr[:, :], qgrow[:, :], start=True, stop=True))
        ten(t0)
        def v0c(v):
            seq(v.tensor_copy(anegb[:, :], pinit[:, 0:64]))
            seq(v.tensor_copy(qgb[:, :], pinit[:, 64:80]))
        vec(v0c)

    with nc.sbuf_tensor("xnfm", [128, 16 * T], BF16) as xnfm:
        with nc.sbuf_tensor("xt0", [128, D], F32) as xt0, \
             nc.sbuf_tensor("xn0", [128, D], F32) as xn0, \
             nc.psum_tensor("pTT0", [128, 2048], F32) as pTT0:
            for tc in range(NTC):
                def dl(s, tc=tc):
                    dma(s, xt0[:, :], X[128 * tc:128 * (tc + 1), :])
                dms(dl)
                ln_emit(xt0, xn0, pTT0)
                def t1(t, tc=tc):
                    for j in range(16):
                        seq(t.transpose(pTT0[:, 128 * j:128 * (j + 1)], xn0[:, 128 * j:128 * (j + 1)], i128[:, :]))
                ten(t1)
                def v1(v, tc=tc):
                    for j in range(16):
                        seq(v.tensor_copy(xnfm[:, T * j + 128 * tc: T * j + 128 * (tc + 1)],
                                          pTT0[:, 128 * j:128 * (j + 1)]))
                vec(v1)

        with nc.sbuf_tensor("wts", [128, 16 * 512], BF16) as wts, \
             nc.sbuf_tensor("zb", [128, 512], BF16) as zb, \
             nc.sbuf_tensor("xb", [128, 512], F32) as xb, \
             nc.sbuf_tensor("tb", [128, 512], BF16) as tb, \
             nc.psum_tensor("pA0", [128, 512], F32) as pA0, \
             nc.psum_tensor("pT0", [128, 512], F32) as pT0:
            for ot in range(17):
                ow = 512 if ot < 16 else 192
                def wl(gp, ot=ot, ow=ow):
                    for dc in range(16):
                        dma(gp, wts[:, 512 * dc:512 * dc + ow], WinT[128 * dc:128 * (dc + 1), 512 * ot:512 * ot + ow])
                dmg(wl)
                for tc in range(NTC):
                    def mm(t, ot=ot, ow=ow, tc=tc):
                        for dc in range(16):
                            seq(t.matmul(pA0[:, 0:ow], xnfm[:, T * dc + 128 * tc: T * dc + 128 * (tc + 1)],
                                         wts[:, 512 * dc:512 * dc + ow], start=(dc == 0), stop=(dc == 15)))
                    ten(mm)
                    if ot < 8:
                        def ev(v, tc=tc):
                            seq(v.tensor_copy(zb[:, :], pA0[:, 0:512]))
                        vec(ev)
                        def dw(s, ot=ot, tc=tc):
                            dma(s, zsp[128 * tc:128 * (tc + 1), 512 * ot:512 * (ot + 1)], zb[:, :])
                        dms(dw)
                    elif ot < 16:
                        def ev(v, tc=tc):
                            seq(v.tensor_copy(xb[:, :], pA0[:, 0:512]))
                        vec(ev)
                        def tr(t, tc=tc):
                            for j in range(4):
                                seq(t.transpose(pT0[:, 128 * j:128 * (j + 1)], xb[:, 128 * j:128 * (j + 1)], i128[:, :]))
                        ten(tr)
                        def ev2(v, tc=tc):
                            seq(v.tensor_copy(tb[:, :], pT0[:, :]))
                        vec(ev2)
                        def dw(s, ot=ot, tc=tc):
                            for j in range(4):
                                ch = (ot - 8) * 4 + j
                                dma(s, xbcf[128 * ch:128 * (ch + 1), 128 * tc:128 * (tc + 1)], tb[:, 128 * j:128 * (j + 1)])
                        dms(dw)
                    else:
                        def ev(v, tc=tc):
                            seq(v.tensor_copy(xb[:, 0:192], pA0[:, 0:192]))
                        vec(ev)
                        def tr(t, tc=tc):
                            seq(t.transpose(pT0[:, 0:128], xb[:, 0:128], i128[:, :]))
                        ten(tr)
                        def ev2(v, tc=tc):
                            seq(v.tensor_copy(tb[:, 0:128], pT0[:, 0:128]))
                        vec(ev2)
                        def dw(s, tc=tc):
                            dma(s, xbcf[4096:4224, 128 * tc:128 * (tc + 1)], tb[:, 0:128])
                            dma(s, dtsp[128 * tc:128 * (tc + 1), :], xb[:, 128:192])
                        dms(dw)

        with nc.sbuf_tensor("wgq4", [128, 4 * 2048], BF16) as wgq4, \
             nc.sbuf_tensor("wuq4", [128, 4 * 2048], BF16) as wuq4, \
             nc.sbuf_tensor("gs1", [128, 512], F32) as gs1, \
             nc.sbuf_tensor("hb1", [128, 2048], BF16) as hb1, \
             nc.sbuf_tensor("ab1", [128, 2048], BF16) as ab1, \
             nc.sbuf_tensor("gg1", [128, 6 * 512], BF16) as gg1, \
             nc.sbuf_tensor("ww1", [128, 2 * 512], BF16) as ww1, \
             nc.sbuf_tensor("msk1", [128, 512], BF16) as msk1, \
             nc.sbuf_tensor("hsb1", [128, 2048], BF16) as hsb1, \
             nc.psum_tensor("pA2", [128, 512], F32) as pA2, \
             nc.psum_tensor("pB2", [128, 512], F32) as pB2:
            for qd in range(16):
                def wl(gp, qd=qd):
                    for m in range(4):
                        col = m * 2048 + 128 * qd
                        for dc in range(16):
                            dma(gp, wgq4[:, 2048 * m + 128 * dc:2048 * m + 128 * (dc + 1)],
                                WG0[128 * dc:128 * (dc + 1), col:col + 128])
                            dma(gp, wuq4[:, 2048 * m + 128 * dc:2048 * m + 128 * (dc + 1)],
                                WU0[128 * dc:128 * (dc + 1), col:col + 128])
                dmg(wl)
                for ts5 in range(8):
                    for m in range(4):
                        def mm(t, qd=qd, ts5=ts5, m=m):
                            for dc in range(16):
                                rhs = xnfm[:, T * dc + 512 * ts5: T * dc + 512 * (ts5 + 1)]
                                seq(t.matmul(pA2[:, :], wgq4[:, 2048 * m + 128 * dc:2048 * m + 128 * (dc + 1)], rhs,
                                             start=(dc == 0), stop=(dc == 15)))
                                seq(t.matmul(pB2[:, :], wuq4[:, 2048 * m + 128 * dc:2048 * m + 128 * (dc + 1)], rhs,
                                             start=(dc == 0), stop=(dc == 15)))
                        ten(mm)
                        def a1(a, m=m):
                            seq(a.activation(gs1[:, :], pA2[:, :], AF.Silu))
                        act(a1)
                        def v1(v, m=m):
                            seq(v.tensor_tensor(hb1[:, 512 * m:512 * (m + 1)], gs1[:, :], pB2[:, :], ALU.mult))
                        vec(v1)
                        def a2(a, m=m):
                            seq(a.activation(ab1[:, 512 * m:512 * (m + 1)], hb1[:, 512 * m:512 * (m + 1)], AF.Abs))
                        act(a2)
                    def v2(v):
                        A0 = ab1[:, 0:512]; A1 = ab1[:, 512:1024]; A2_ = ab1[:, 1024:1536]; A3 = ab1[:, 1536:2048]
                        G = lambda i: gg1[:, 512 * i:512 * (i + 1)]
                        seq(v.tensor_tensor(G(0), A0, A1, ALU.is_ge))
                        seq(v.tensor_tensor(G(1), A0, A2_, ALU.is_ge))
                        seq(v.tensor_tensor(G(2), A0, A3, ALU.is_ge))
                        seq(v.tensor_tensor(G(3), A1, A2_, ALU.is_ge))
                        seq(v.tensor_tensor(G(4), A1, A3, ALU.is_ge))
                        seq(v.tensor_tensor(G(5), A2_, A3, ALU.is_ge))
                        v.drain()
                        W0 = ww1[:, 0:512]; W1 = ww1[:, 512:1024]
                        seq(v.tensor_tensor(W0, G(0), G(1), ALU.add))
                        v.drain()
                        seq(v.tensor_tensor(W1, W0, G(2), ALU.add))
                        v.drain()
                        seq(v.tensor_scalar(msk1[:, :], W1, 2.0, None, ALU.is_ge))
                        v.drain()
                        seq(v.tensor_tensor(hsb1[:, 0:512], hb1[:, 0:512], msk1[:, :], ALU.mult))
                        seq(v.scalar_tensor_tensor(W0, G(3), 1.0, G(0), ALU.add, ALU.subtract))
                        v.drain()
                        seq(v.tensor_tensor(W1, W0, G(4), ALU.add))
                        v.drain()
                        seq(v.tensor_scalar(msk1[:, :], W1, 2.0, None, ALU.is_ge))
                        v.drain()
                        seq(v.tensor_tensor(hsb1[:, 512:1024], hb1[:, 512:1024], msk1[:, :], ALU.mult))
                        seq(v.scalar_tensor_tensor(W0, G(5), 2.0, G(1), ALU.add, ALU.subtract))
                        v.drain()
                        seq(v.tensor_tensor(W1, W0, G(3), ALU.subtract))
                        v.drain()
                        seq(v.tensor_scalar(msk1[:, :], W1, 2.0, None, ALU.is_ge))
                        v.drain()
                        seq(v.tensor_tensor(hsb1[:, 1024:1536], hb1[:, 1024:1536], msk1[:, :], ALU.mult))
                        seq(v.tensor_tensor(W0, G(2), G(4), ALU.add))
                        v.drain()
                        seq(v.tensor_tensor(W1, W0, G(5), ALU.add))
                        v.drain()
                        seq(v.tensor_scalar(msk1[:, :], W1, 1.0, None, ALU.is_le))
                        v.drain()
                        seq(v.tensor_tensor(hsb1[:, 1536:2048], hb1[:, 1536:2048], msk1[:, :], ALU.mult))
                    vec(v2)
                    def dw(s, qd=qd, ts5=ts5):
                        for m in range(4):
                            row = m * 2048 + 128 * qd
                            dma(s, hsp[row:row + 128, 512 * ts5:512 * (ts5 + 1)], hsb1[:, 512 * m:512 * (m + 1)])
                    dms(dw)

    with nc.sbuf_tensor("whalf", [128, 64 * 1024], BF16) as whalf, \
         nc.sbuf_tensor("htl", [128, 512], BF16) as htl, \
         nc.sbuf_tensor("gb0", [128, 1024], BF16) as gb0, \
         nc.psum_tensor("pD8", [128, 4096], F32) as pD8:
        for dh in range(2):
            def wl(gp, dh=dh):
                for c in range(64):
                    dma(gp, whalf[:, 1024 * c:1024 * (c + 1)], WD0[128 * c:128 * (c + 1), 1024 * dh:1024 * (dh + 1)])
            dmg(wl)
            for t4 in range(8):
                for c in range(64):
                    def dl(s, t4=t4, c=c):
                        dma(s, htl[:, :], hsp[128 * c:128 * (c + 1), 512 * t4:512 * (t4 + 1)])
                    dms(dl)
                    def mm(t, c=c):
                        for tsub in range(4):
                            for dj in range(2):
                                seq(t.matmul(pD8[:, 1024 * tsub + 512 * dj:1024 * tsub + 512 * (dj + 1)],
                                             htl[:, 128 * tsub:128 * (tsub + 1)],
                                             whalf[:, 1024 * c + 512 * dj:1024 * c + 512 * (dj + 1)],
                                             start=(c == 0), stop=(c == 63)))
                    ten(mm)
                for tsub in range(4):
                    def a1(a, tsub=tsub):
                        seq(a.activation(gb0[:, :], pD8[:, 1024 * tsub:1024 * (tsub + 1)], AF.Silu))
                    act(a1)
                    def dw(s, dh=dh, t4=t4, tsub=tsub):
                        row = 512 * t4 + 128 * tsub
                        dma(s, g0sp[row:row + 128, 1024 * dh:1024 * (dh + 1)], gb0[:, :])
                    dms(dw)

    with ExitStack() as st5:
        E5 = st5.enter_context
        DIall = E5(nc.sbuf_tensor("DIall", [128, 64 * 128], BF16))
        Sa = E5(nc.sbuf_tensor("Sa", [64, 64 * 64], BF16))
        Sb = E5(nc.sbuf_tensor("Sb", [64, 64 * 64], BF16))
        Ssc = E5(nc.sbuf_tensor("Ssc", [64, 8 * 64], BF16))
        ib64 = E5(nc.sbuf_tensor("ib64", [64, 64], BF16))
        xbcs = E5(nc.sbuf_tensor("xbcs", [128, 33 * 131], BF16))
        xhsf = E5(nc.sbuf_tensor("xhsf", [128, 33 * 128], F32))
        bc32 = E5(nc.sbuf_tensor("bc32", [128, 128], BF16))
        xhtm = E5(nc.sbuf_tensor("xhtm", [128, DIN], BF16))
        xttm = E5(nc.sbuf_tensor("xttm", [128, DIN], BF16))
        Bbar = E5(nc.sbuf_tensor("Bbar", [128, 64 * 64], BF16))
        Pall = E5(nc.sbuf_tensor("Pall", [128, 16 * 128], BF16))
        ytm = E5(nc.sbuf_tensor("ytm", [128, DIN], BF16))
        ztl = E5(nc.sbuf_tensor("ztl", [128, DIN], BF16))
        zsl = E5(nc.sbuf_tensor("zsl", [128, DIN], BF16))
        ebuf = E5(nc.sbuf_tensor("ebuf", [128, 2048], F32))
        ebb = E5(nc.sbuf_tensor("ebb", [128, 2048], BF16))
        m0m = E5(nc.sbuf_tensor("m0m", [128, 128], BF16))
        cvA = E5(nc.sbuf_tensor("cvA", [128, 128], F32))
        cvB = E5(nc.sbuf_tensor("cvB", [128, 128], F32))
        cvS = E5(nc.sbuf_tensor("cvS", [128, 128], F32))
        dtr = E5(nc.sbuf_tensor("dtr", [128, 128], F32))
        dtfm = E5(nc.sbuf_tensor("dtfm", [128, 128], F32))
        dttm = E5(nc.sbuf_tensor("dttm", [128, 64], F32))
        atm = E5(nc.sbuf_tensor("atm", [128, 64], BF16))
        catm = E5(nc.sbuf_tensor("catm", [128, 64], F32))
        cafm = E5(nc.sbuf_tensor("cafm", [128, 128], F32))
        etm = E5(nc.sbuf_tensor("etm", [128, 64], F32))
        e2fm = E5(nc.sbuf_tensor("e2fm", [128, 128], F32))
        e2tm = E5(nc.sbuf_tensor("e2tm", [128, 64], F32))
        btm = E5(nc.sbuf_tensor("btm", [128, 64], BF16))
        dab = E5(nc.sbuf_tensor("dab", [64, 64], F32))
        uscr = E5(nc.sbuf_tensor("uscr", [128, 64], F32))
        yssq = E5(nc.sbuf_tensor("yssq", [128, 1], F32))
        yr = E5(nc.sbuf_tensor("yr", [128, 1], F32))
        carow = E5(nc.sbuf_tensor("carow", [1, 64 * 128], F32))
        cfm = E5(nc.sbuf_tensor("cfm", [64, 128], BF16))
        calst = E5(nc.sbuf_tensor("calst", [1, 64], F32))
        elst = E5(nc.sbuf_tensor("elst", [1, 64], F32))
        pE = E5(nc.psum_tensor("pE", [128, 2048], F32))
        pSc = E5(nc.psum_tensor("pSc", [128, 512], F32))
        pY = E5(nc.psum_tensor("pY", [128, 512], F32))
        pU = E5(nc.psum_tensor("pU", [128, 512], F32))
        pS = E5(nc.psum_tensor("pS", [128, 512], F32))
        ynf = xhsf[:, 0:DIN]
        def t2(t):
            seq(t.matmul(pE[:, 0:64], onesr[:, :], drow[:, :], start=True, stop=True))
        ten(t2)
        def v4(v):
            seq(v.tensor_copy(ebuf[:, 0:64], pE[:, 0:64]))
            seq(v.tensor_copy(ib64[:, :], i128[0:64, 0:64]))
            v.drain()
            for h in range(NHm):
                seq(v.tensor_scalar(DIall[:, 128 * h:128 * (h + 1)], i128[:, :], ebuf[:, h:h + 1], None, ALU.mult))
            seq(v.memset(Sa[:, :], 0.0))
            seq(v.memset(xbcs[:, :], 0.0))
        vec(v4)

        for ch in range(NTC):
            cs = Sa if ch % 2 == 0 else Sb
            ns = Sb if ch % 2 == 0 else Sa
            def dl(s, ch=ch):
                if ch == 0:
                    for i in range(33):
                        dma(s, xbcs[:, 131 * i + 3:131 * (i + 1)], xbcf[128 * i:128 * (i + 1), 0:128])
                else:
                    for i in range(33):
                        dma(s, xbcs[:, 131 * i:131 * (i + 1)], xbcf[128 * i:128 * (i + 1), 128 * ch - 3:128 * (ch + 1)])
                dma(s, ztl[:, :], zsp[128 * ch:128 * (ch + 1), :])
                dma(s, dtr[:, 0:64], dtsp[128 * ch:128 * (ch + 1), :])
            dms(dl)
            if ch == 0:
                def vz(v):
                    for i in range(33):
                        seq(v.memset(xbcs[:, 131 * i:131 * i + 3], 0.0))
                vec(vz)
            def vc(v, ch=ch):
                for i in range(33):
                    xbi = xbcs[:, 131 * i:131 * (i + 1)]
                    seq(v.tensor_scalar(cvA[:, :], xbi[:, 0:128], cw[:, 4 * i:4 * i + 1], None, ALU.mult))
                    seq(v.tensor_scalar(cvB[:, :], xbi[:, 1:129], cw[:, 4 * i + 1:4 * i + 2], None, ALU.mult))
                    v.drain()
                    seq(v.tensor_tensor(cvS[:, :], cvA[:, :], cvB[:, :], ALU.add))
                    seq(v.tensor_scalar(cvA[:, :], xbi[:, 2:130], cw[:, 4 * i + 2:4 * i + 3], None, ALU.mult))
                    v.drain()
                    seq(v.tensor_tensor(cvS[:, :], cvS[:, :], cvA[:, :], ALU.add))
                    seq(v.tensor_scalar(cvA[:, :], xbi[:, 3:131], cw[:, 4 * i + 3:4 * i + 4], None, ALU.mult))
                    v.drain()
                    seq(v.tensor_tensor(xhsf[:, 128 * i:128 * (i + 1)], cvS[:, :], cvA[:, :], ALU.add))
            vec(vc)
            def ac(a, ch=ch):
                for i in range(33):
                    seq(a.activation(xhsf[:, 128 * i:128 * (i + 1)], xhsf[:, 128 * i:128 * (i + 1)], AF.Silu,
                                     bias=cb[:, i:i + 1]))
            act(ac)
            def vbc(v):
                seq(v.tensor_copy(bc32[:, :], xhsf[:, 32 * 128:33 * 128]))
            vec(vbc)
            def dcf(s):
                dma(s, cfm[:, :], bc32[64:128, :])
            dms(dcf)
            for rr in range(8):
                def tr(t, rr=rr):
                    for j in range(4):
                        i = 4 * rr + j
                        seq(t.transpose(pSc[:, 128 * j:128 * (j + 1)], xhsf[:, 128 * i:128 * (i + 1)], i128[:, :]))
                ten(tr)
                def ve(v, rr=rr):
                    for j in range(4):
                        i = 4 * rr + j
                        seq(v.tensor_copy(xhtm[:, 128 * i:128 * (i + 1)], pSc[:, 128 * j:128 * (j + 1)]))
                vec(ve)
            def tr2(t):
                seq(t.transpose(pSc[:, 0:128], xhsf[:, 32 * 128:33 * 128], i128[:, :]))
                seq(t.transpose(pSc[:, 128:256], dtr[:, :], i128[:, :]))
            ten(tr2)
            def ve2(v):
                seq(v.tensor_copy(btm[:, :], pSc[:, 0:64]))
            vec(ve2)
            def vd0(v):
                seq(v.tensor_scalar(cvA[0:64, :], pSc[0:64, 128:256], dtb[0:64, :], None, ALU.add))
            vec(vd0)
            def ad(a):
                seq(a.activation(cvB[0:64, :], cvA[0:64, :], AF.Exp))
                a.drain()
                seq(a.activation(dtfm[0:64, :], cvB[0:64, :], AF.Ln, bias=1.0))
            act(ad)
            def tr3(t):
                seq(t.transpose(pSc[:, 256:384], dtfm[:, :], i128[:, :]))
            ten(tr3)
            def ve3(v):
                seq(v.tensor_copy(dttm[:, :], pSc[:, 256:320]))
                v.drain()
                seq(v.tensor_tensor(atm[:, :], dttm[:, :], anegb[:, :], ALU.mult))
            vec(ve3)
            def tr4(t):
                seq(t.matmul(pSc[:, 384:448], trib[:, :], atm[:, :], start=True, stop=True))
            ten(tr4)
            def ve4(v):
                seq(v.tensor_copy(catm[:, :], pSc[:, 384:448]))
            vec(ve4)
            def ae(a):
                seq(a.activation(etm[:, :], catm[:, :], AF.Exp))
            act(ae)
            def ve5(v):
                seq(v.tensor_copy(dtfm[:, 0:64], catm[:, :]))
            vec(ve5)
            def tr6(t):
                seq(t.transpose(pSc[:, 0:128], dtfm[:, :], i128[:, :]))
            ten(tr6)
            def ve6(v):
                seq(v.tensor_copy(cafm[0:64, :], pSc[0:64, 0:128]))
            vec(ve6)
            def ae2(a):
                seq(a.activation(e2fm[0:64, :], cafm[0:64, :], AF.Exp, bias=cafm[0:64, 127:128], scale=-1.0))
            act(ae2)
            def dca(s):
                dma(s, carow[0:1, :], cafm[0:64, :])
                dma(s, calst[0:1, :], cafm[0:64, 127:128])
            dms(dca)
            def ael(a):
                seq(a.activation(elst[:, :], calst[:, :], AF.Exp))
            act(ael)
            def tr7(t):
                seq(t.transpose(pSc[:, 128:256], e2fm[:, :], i128[:, :]))
            ten(tr7)
            def ve7(v):
                seq(v.tensor_copy(e2tm[:, :], pSc[:, 128:192]))
                for h in range(NHm):
                    seq(v.tensor_scalar(xttm[:, 64 * h:64 * (h + 1)], xhtm[:, 64 * h:64 * (h + 1)],
                                        dttm[:, h:h + 1], None, ALU.mult))
                v.drain()
                for h in range(NHm):
                    seq(v.tensor_scalar(Bbar[:, 64 * h:64 * (h + 1)], btm[:, :], e2tm[:, h:h + 1], None, ALU.mult))
            vec(ve7)
            def tr8(t):
                seq(t.matmul(pSc[0:64, 256:320], onesr[:, 0:64], elst[:, :], start=True, stop=True))
            ten(tr8)
            def ve8(v):
                seq(v.tensor_copy(dab[:, :], pSc[0:64, 256:320]))
            vec(ve8)
            def tm0(t):
                seq(t.matmul(pSc[:, 384:512], bc32[0:64, :], cfm[:, :], start=True, stop=True))
            ten(tm0)
            def vm0(v):
                seq(v.tensor_tensor(m0m[:, :], pSc[:, 384:512], tri[:, :], ALU.mult))
            vec(vm0)
            for hg in range(4):
                def te(t, hg=hg):
                    for j in range(16):
                        h = 16 * hg + j
                        seq(t.matmul(pE[:, 128 * j:128 * (j + 1)], onesr[:, :],
                                     carow[0:1, 128 * h:128 * (h + 1)], start=True, stop=True))
                ten(te)
                def vee(v, hg=hg):
                    for j in range(16):
                        h = 16 * hg + j
                        seq(v.tensor_scalar(ebuf[:, 128 * j:128 * (j + 1)], pE[:, 128 * j:128 * (j + 1)],
                                            catm[:, h:h + 1], 0.0, ALU.subtract, ALU.min))
                vec(vee)
                def aee(a, hg=hg):
                    seq(a.activation(ebb[:, :], ebuf[:, :], AF.Exp))
                act(aee)
                def vpp(v, hg=hg):
                    for j in range(16):
                        h = 16 * hg + j
                        seq(v.tensor_tensor(Pall[:, 128 * j:128 * (j + 1)], ebb[:, 128 * j:128 * (j + 1)],
                                            m0m[:, :], ALU.mult))
                vec(vpp)
                for sg in range(2):
                    def vs(v, hg=hg, sg=sg, cs=cs):
                        for j in range(8):
                            h = 16 * hg + 8 * sg + j
                            seq(v.tensor_scalar(Ssc[:, 64 * j:64 * (j + 1)], cs[:, 64 * h:64 * (h + 1)],
                                                dab[:, h:h + 1], None, ALU.mult))
                    vec(vs)
                    def th(t, hg=hg, sg=sg, cs=cs):
                        for j in range(8):
                            h = 16 * hg + 8 * sg + j
                            jl = 8 * sg + j
                            sl = slice(64 * j, 64 * (j + 1))
                            seq(t.matmul(pY[:, sl], Pall[:, 128 * jl:128 * (jl + 1)], xttm[:, 64 * h:64 * (h + 1)],
                                         start=True, stop=False))
                            seq(t.matmul(pY[:, sl], DIall[:, 128 * h:128 * (h + 1)], xhtm[:, 64 * h:64 * (h + 1)],
                                         start=False, stop=True))
                            seq(t.matmul(pU[:, sl], cfm[:, :], cs[:, 64 * h:64 * (h + 1)], start=True, stop=True))
                            seq(t.matmul(pS[0:64, sl], ib64[:, :], Ssc[:, sl], start=True, stop=False))
                            seq(t.matmul(pS[0:64, sl], Bbar[:, 64 * h:64 * (h + 1)], xttm[:, 64 * h:64 * (h + 1)],
                                         start=False, stop=True))
                    ten(th)
                    def vh(v, hg=hg, sg=sg, ns=ns):
                        for j in range(8):
                            h = 16 * hg + 8 * sg + j
                            sl = slice(64 * j, 64 * (j + 1))
                            seq(v.tensor_scalar(uscr[:, 0:64], pU[:, sl], etm[:, h:h + 1], None, ALU.mult))
                            v.drain()
                            seq(v.tensor_tensor(ytm[:, 64 * h:64 * (h + 1)], uscr[:, 0:64], pY[:, sl], ALU.add))
                            seq(v.tensor_copy(ns[:, 64 * h:64 * (h + 1)], pS[0:64, sl]))
                    vec(vh)
            def az(a):
                seq(a.activation(zsl[:, :], ztl[:, :], AF.Silu))
            act(az)
            def vy(v):
                seq(v.tensor_tensor(ynf[:, :], ytm[:, :], zsl[:, :], ALU.mult))
            vec(vy)
            def ay(a):
                seq(a.activation(ztl[:, :], ynf[:, :], AF.Square, accum_out=yssq[:, :]))
            act(ay)
            def vy1b(v):
                seq(v.tensor_scalar(yssq[:, :], yssq[:, :], 1.0 / DIN, 1e-5, ALU.mult, ALU.add))
            vec(vy1b)
            def ay2(a):
                seq(a.activation(yssq[:, :], yssq[:, :], AF.Sqrt))
            act(ay2)
            def vy2(v):
                seq(v.reciprocal(yr[:, :], yssq[:, :]))
                v.drain()
                seq(v.tensor_scalar(ynf[:, :], ynf[:, :], yr[:, :], None, ALU.mult))
            vec(vy2)
            for rr in range(8):
                def tq(t, rr=rr):
                    for j in range(4):
                        i = 4 * rr + j
                        seq(t.transpose(pSc[:, 128 * j:128 * (j + 1)], ynf[:, 128 * i:128 * (i + 1)], i128[:, :]))
                ten(tq)
                def vq(v, rr=rr):
                    for j in range(4):
                        seq(v.tensor_copy(ztl[:, 128 * (4 * rr + j):128 * (4 * rr + j + 1)],
                                          pSc[:, 128 * j:128 * (j + 1)]))
                vec(vq)
            def dyw(s, ch=ch):
                for i in range(32):
                    dma(s, ynsp[128 * i:128 * (i + 1), 128 * ch:128 * (ch + 1)], ztl[:, 128 * i:128 * (i + 1)])
            dms(dyw)

    with nc.sbuf_tensor("wouts", [128, 32 * 2048], BF16) as wouts, \
         nc.sbuf_tensor("yft", [128, 32 * 128], BF16) as yft, \
         nc.sbuf_tensor("g0f", [128, D], F32) as g0f, \
         nc.sbuf_tensor("g0b", [128, D], BF16) as g0b, \
         nc.sbuf_tensor("xtl", [128, D], F32) as xtl, \
         nc.sbuf_tensor("x1b", [128, D], F32) as x1b, \
         nc.sbuf_tensor("sscr", [128, 512], F32) as sscr, \
         nc.sbuf_tensor("nwc", [128, 32], F32) as nwc, \
         nc.psum_tensor("pD1", [128, 512], F32) as pD1:
        def wl(gp):
            for c in range(32):
                dma(gp, wouts[:, 2048 * c:2048 * (c + 1)], WoutT[128 * c:128 * (c + 1), :])
        dmg(wl)
        def dnw(s):
            dma(s, nwc[:, :], NWC)
        dms(dnw)
        def vnw(v):
            for c in range(32):
                seq(v.tensor_scalar(wouts[:, 2048 * c:2048 * (c + 1)], wouts[:, 2048 * c:2048 * (c + 1)],
                                    nwc[:, c:c + 1], None, ALU.mult))
        vec(vnw)
        for tc in range(NTC):
            def dl(s, tc=tc):
                for c in range(32):
                    dma(s, yft[:, 128 * c:128 * (c + 1)], ynsp[128 * c:128 * (c + 1), 128 * tc:128 * (tc + 1)])
                dma(s, g0b[:, :], g0sp[128 * tc:128 * (tc + 1), :])
                dma(s, xtl[:, :], X[128 * tc:128 * (tc + 1), :])
            dms(dl)
            def vg(v):
                seq(v.tensor_copy(g0f[:, :], g0b[:, :]))
            vec(vg)
            for dt4 in range(4):
                def mm(t, dt4=dt4):
                    for c in range(32):
                        seq(t.matmul(pD1[:, :], yft[:, 128 * c:128 * (c + 1)],
                                     wouts[:, 2048 * c + 512 * dt4:2048 * c + 512 * (dt4 + 1)],
                                     start=(c == 0), stop=(c == 31)))
                ten(mm)
                def vv(v, dt4=dt4):
                    seq(v.tensor_tensor(sscr[:, :], pD1[:, :], g0f[:, 512 * dt4:512 * (dt4 + 1)], ALU.mult))
                    v.drain()
                    seq(v.tensor_tensor(x1b[:, 512 * dt4:512 * (dt4 + 1)], sscr[:, :],
                                        xtl[:, 512 * dt4:512 * (dt4 + 1)], ALU.add))
                vec(vv)
            def dw(s, tc=tc):
                dma(s, x1sp[128 * tc:128 * (tc + 1), :], x1b[:, :])
            dms(dw)

    with nc.sbuf_tensor("xn1fm", [128, 16 * T], BF16) as xn1fm, \
         nc.sbuf_tensor("xt1", [128, D], F32) as xt1, \
         nc.sbuf_tensor("xn1", [128, D], F32) as xn1, \
         nc.sbuf_tensor("xqt", [128, D], F32) as xqt, \
         nc.sbuf_tensor("xqb", [128, D], BF16) as xqb, \
         nc.sbuf_tensor("am1", [128, 1], F32) as am1, \
         nc.sbuf_tensor("sc1", [128, 1], F32) as sc1, \
         nc.psum_tensor("pTT1", [128, 2048], F32) as pTT1:
        for tc in range(NTC):
            def dl(s, tc=tc):
                dma(s, xt1[:, :], x1sp[128 * tc:128 * (tc + 1), :])
            dms(dl)
            ln_emit(xt1, xn1, pTT1)
            def t1(t, tc=tc):
                for j in range(16):
                    seq(t.transpose(pTT1[:, 128 * j:128 * (j + 1)], xn1[:, 128 * j:128 * (j + 1)], i128[:, :]))
            ten(t1)
            def v1(v, tc=tc):
                for j in range(16):
                    seq(v.tensor_copy(xn1fm[:, T * j + 128 * tc:T * j + 128 * (tc + 1)], pTT1[:, 128 * j:128 * (j + 1)]))
                seq(v.tensor_reduce(am1[:, :], xn1[:, :], AX.X, ALU.max, apply_absolute_value=True))
                v.drain()
                seq(v.tensor_scalar(amax_all[:, tc:tc + 1], am1[:, :], 1e-5, None, ALU.max))
            vec(v1)
            def v2(v, tc=tc):
                seq(v.reciprocal(sc1[:, :], amax_all[:, tc:tc + 1]))
                v.drain()
                seq(v.tensor_scalar(sc1[:, :], sc1[:, :], 127.0, None, ALU.mult))
                seq(v.tensor_scalar(amaxs[:, tc:tc + 1], amax_all[:, tc:tc + 1], 1.0 / 127.0, None, ALU.mult))
                v.drain()
                seq(v.tensor_scalar(xqt[:, :], xn1[:, :], sc1[:, :], None, ALU.mult))
                v.drain()
                seq(v.tensor_scalar(xqt[:, :], xqt[:, :], MAGIC, MAGIC, ALU.add, ALU.subtract))
            vec(v2)
            def t2(t, tc=tc):
                for j in range(16):
                    seq(t.transpose(pTT1[:, 128 * j:128 * (j + 1)], xqt[:, 128 * j:128 * (j + 1)], i128[:, :]))
            ten(t2)
            def v3(v, tc=tc):
                for j in range(16):
                    seq(v.tensor_copy(xqb[:, 128 * j:128 * (j + 1)], pTT1[:, 128 * j:128 * (j + 1)]))
            vec(v3)
            def dw(s, tc=tc):
                for j in range(16):
                    dma(s, xqsp[128 * j:128 * (j + 1), 128 * tc:128 * (tc + 1)], xqb[:, 128 * j:128 * (j + 1)])
            dms(dw)

        with nc.sbuf_tensor("wst", [128, 16 * 512], BF16) as wst, \
             nc.sbuf_tensor("qb1", [128, 128], F32) as qb1, \
             nc.sbuf_tensor("rsc", [128, 32], F32) as rsc, \
             nc.sbuf_tensor("sqj", [128, 512], BF16) as sqj, \
             nc.sbuf_tensor("ssh", [128, 1], F32) as ssh, \
             nc.sbuf_tensor("rh", [128, 1], F32) as rh, \
             nc.sbuf_tensor("qout", [128, 512], BF16) as qout, \
             nc.psum_tensor("pA3", [128, 512], F32) as pA3, \
             nc.psum_tensor("pT3", [128, 512], F32) as pT3:
            for s6 in range(6):
                W6 = [WQT, WQT, WQT, WQT, WKT, WVT][s6]
                co = [0, 512, 1024, 1536, 0, 0][s6]
                def wl(gp, W6=W6, co=co):
                    for dc in range(16):
                        dma(gp, wst[:, 512 * dc:512 * (dc + 1)], W6[128 * dc:128 * (dc + 1), co:co + 512])
                dmg(wl)
                for tc in range(NTC):
                    def mm(t, tc=tc):
                        for dc in range(16):
                            seq(t.matmul(pA3[:, :], xn1fm[:, T * dc + 128 * tc:T * dc + 128 * (tc + 1)],
                                         wst[:, 512 * dc:512 * (dc + 1)], start=(dc == 0), stop=(dc == 15)))
                    ten(mm)
                    if s6 == 5:
                        def ve(v):
                            seq(v.tensor_copy(qout[:, :], pA3[:, :]))
                        vec(ve)
                        def dw(s, tc=tc):
                            dma(s, vtsp[128 * tc:128 * (tc + 1), :], qout[:, :])
                        dms(dw)
                    else:
                        for j in range(4):
                            hd = slice(128 * j, 128 * (j + 1))
                            def a1(a, j=j, hd=hd):
                                seq(a.activation(sqj[:, 0:128], pA3[:, hd], AF.Square, accum_out=ssh[:, :]))
                            act(a1)
                            def v0q(v):
                                seq(v.tensor_scalar(ssh[:, :], ssh[:, :], 1.0 / 128.0, 1.1920929e-7, ALU.mult, ALU.add))
                            vec(v0q)
                            def a1b(a):
                                seq(a.activation(ssh[:, :], ssh[:, :], AF.Sqrt))
                            act(a1b)
                            def v1(v, j=j, hd=hd, s6=s6, tc=tc):
                                seq(v.reciprocal(rh[:, :], ssh[:, :]))
                                v.drain()
                                seq(v.tensor_scalar(qb1[:, :], pA3[:, hd], rh[:, :], None, ALU.mult))
                                v.drain()
                                ct = cost[:, 8 * tc:8 * (tc + 1)]
                                st_ = sint[:, 8 * tc:8 * (tc + 1)]
                                seq(v.tensor_tensor(rsc[:, 0:8], qb1[:, 0:8], ct, ALU.mult))
                                seq(v.tensor_tensor(rsc[:, 8:16], qb1[:, 8:16], st_, ALU.mult))
                                seq(v.tensor_tensor(rsc[:, 16:24], qb1[:, 8:16], ct, ALU.mult))
                                seq(v.tensor_tensor(rsc[:, 24:32], qb1[:, 0:8], st_, ALU.mult))
                                v.drain()
                                seq(v.tensor_tensor(qb1[:, 0:8], rsc[:, 0:8], rsc[:, 8:16], ALU.add))
                                seq(v.tensor_tensor(qb1[:, 8:16], rsc[:, 24:32], rsc[:, 16:24], ALU.subtract))
                                v.drain()
                                if s6 < 4:
                                    h = 4 * s6 + j
                                    seq(v.tensor_scalar(qb1[:, :], qb1[:, :], qgb[:, h:h + 1], RT128, ALU.mult, ALU.mult))
                                    v.drain()
                            vec(v1)
                            def t2(t, j=j):
                                seq(t.transpose(pT3[:, hd], qb1[:, :], i128[:, :]))
                            ten(t2)
                        def v2(v):
                            seq(v.tensor_copy(qout[:, :], pT3[:, :]))
                        vec(v2)
                        def dw(s, s6=s6, tc=tc):
                            tgt = qfsp if s6 < 4 else kfsp
                            ro = 512 * s6 if s6 < 4 else 0
                            for j in range(4):
                                dma(s, tgt[ro + 128 * j:ro + 128 * (j + 1), 128 * tc:128 * (tc + 1)],
                                    qout[:, 128 * j:128 * (j + 1)])
                        dms(dw)

    with nc.sbuf_tensor("qfm", [128, T], BF16) as qfm, \
         nc.sbuf_tensor("kfm", [128, T], BF16) as kfm, \
         nc.sbuf_tensor("vtm", [128, 32 * 128], BF16) as vtm, \
         nc.sbuf_tensor("pbuf", [128, 32 * 512], BF16) as pbuf, \
         nc.sbuf_tensor("dmsk", [128, 4 * 512], BF16) as dmsk, \
         nc.sbuf_tensor("rden", [1, 512], F32) as rden, \
         nc.sbuf_tensor("rdb", [128, 512], F32) as rdb, \
         nc.sbuf_tensor("ynb1", [128, 512], BF16) as ynb1, \
         nc.psum_tensor("pS3", [128, 512], F32) as pS3, \
         nc.psum_tensor("pY3", [128, 512], F32) as pY3, \
         nc.psum_tensor("pd3", [128, 512], F32) as pd3, \
         nc.psum_tensor("pb3", [128, 512], F32) as pb3:
        def dmk(s):
            for r in range(4):
                dma(s, dmsk[:, 512 * r:512 * (r + 1)], DMSK[r])
        dms(dmk)
        for h in range(16):
            kv = h // 4
            def dl(s, h=h, kv=kv):
                dma(s, qfm[:, :], qfsp[128 * h:128 * (h + 1), :])
                if h % 4 == 0:
                    dma(s, kfm[:, :], kfsp[128 * kv:128 * (kv + 1), :])
                    for c in range(32):
                        dma(s, vtm[:, 128 * c:128 * (c + 1)], vtsp[128 * c:128 * (c + 1), 128 * kv:128 * (kv + 1)])
            dms(dl)
            for qt in range(8):
                nkt = 4 * (qt + 1)
                for kt in range(nkt):
                    def mm(t, kt=kt, qt=qt):
                        seq(t.matmul(pS3[:, :], kfm[:, 128 * kt:128 * (kt + 1)],
                                     qfm[:, 512 * qt:512 * (qt + 1)], start=True, stop=True))
                    ten(mm)
                    def a1(a, kt=kt):
                        seq(a.activation(pbuf[:, 512 * kt:512 * (kt + 1)], pS3[:, :], AF.Exp))
                    act(a1)
                def vm(v, qt=qt, nkt=nkt):
                    for r in range(4):
                        kt = nkt - 4 + r
                        seq(v.tensor_tensor(pbuf[:, 512 * kt:512 * (kt + 1)], pbuf[:, 512 * kt:512 * (kt + 1)],
                                            dmsk[:, 512 * r:512 * (r + 1)], ALU.mult))
                vec(vm)
                def mm2(t, qt=qt, nkt=nkt):
                    for kt in range(nkt):
                        seq(t.matmul(pY3[:, :], vtm[:, 128 * kt:128 * (kt + 1)],
                                     pbuf[:, 512 * kt:512 * (kt + 1)], start=(kt == 0), stop=(kt == nkt - 1)))
                    for kt in range(nkt):
                        seq(t.matmul(pd3[0:1, :], onescb[:, :], pbuf[:, 512 * kt:512 * (kt + 1)],
                                     start=(kt == 0), stop=(kt == nkt - 1)))
                ten(mm2)
                def a2(v):
                    seq(v.reciprocal(rden[:, :], pd3[0:1, :]))
                vec(a2)
                def mm3(t):
                    seq(t.matmul(pb3[:, :], onesr[:, :], rden[:, :], start=True, stop=True))
                ten(mm3)
                def vv(v, qt=qt):
                    seq(v.tensor_copy(rdb[:, :], pb3[:, :]))
                    v.drain()
                    seq(v.tensor_tensor(ynb1[:, :], pY3[:, :], rdb[:, :], ALU.mult))
                vec(vv)
                def dw(s, h=h, qt=qt):
                    dma(s, yfsp[128 * h:128 * (h + 1), 512 * qt:512 * (qt + 1)], ynb1[:, :])
                dms(dw)

    with nc.sbuf_tensor("wps", [128, 16 * 2048], BF16) as wps, \
         nc.sbuf_tensor("yin", [128, 16 * 128], BF16) as yin, \
         nc.sbuf_tensor("sout", [128, 2048], BF16) as sout, \
         nc.psum_tensor("pD4", [128, 2048], F32) as pD4:
        def wl(gp):
            for c in range(16):
                dma(gp, wps[:, 2048 * c:2048 * (c + 1)], WPT[128 * c:128 * (c + 1), :])
        dmg(wl)
        for tc in range(NTC):
            def dl(s, tc=tc):
                for c in range(16):
                    dma(s, yin[:, 128 * c:128 * (c + 1)], yfsp[128 * c:128 * (c + 1), 128 * tc:128 * (tc + 1)])
            dms(dl)
            def mm(t, tc=tc):
                for dt4 in range(4):
                    for c in range(16):
                        seq(t.matmul(pD4[:, 512 * dt4:512 * (dt4 + 1)], yin[:, 128 * c:128 * (c + 1)],
                                     wps[:, 2048 * c + 512 * dt4:2048 * c + 512 * (dt4 + 1)],
                                     start=(c == 0), stop=(c == 15)))
            ten(mm)
            def vv(v):
                for dt4 in range(4):
                    seq(v.tensor_copy(sout[:, 512 * dt4:512 * (dt4 + 1)], pD4[:, 512 * dt4:512 * (dt4 + 1)]))
            vec(vv)
            def dw(s, tc=tc):
                dma(s, s1sp[128 * tc:128 * (tc + 1), :], sout[:, :])
            dms(dw)

    with nc.sbuf_tensor("wld", [128, F1], F32) as wld, \
         nc.sbuf_tensor("absw", [128, F1], F32) as absw, \
         nc.sbuf_tensor("sgw", [128, F1], F32) as sgw, \
         nc.sbuf_tensor("screw", [1, F1], F32) as screw, \
         nc.sbuf_tensor("mbw", [128, 512], F32) as mbw, \
         nc.sbuf_tensor("tbw", [128, 512], F32) as tbw, \
         nc.sbuf_tensor("wqb", [128, F1], BF16) as wqb, \
         nc.psum_tensor("pS4", [1, 512], F32) as pS4, \
         nc.psum_tensor("pB4", [128, 512], F32) as pB4:
        for (Wsrc, Wdst, nic, now) in [(WG1, wq1g, 16, F1), (WU1, wq1u, 16, F1), (WD1, wq1d, 43, D)]:
            nol = (now + 511) // 512
            for ic in range(nic):
                def dl(s, Wsrc=Wsrc, ic=ic, now=now):
                    dma(s, wld[:, 0:now], Wsrc[128 * ic:128 * (ic + 1), :])
                dms(dl)
                def a1(a, now=now):
                    seq(a.activation(absw[:, 0:now], wld[:, 0:now], AF.Abs))
                    seq(a.activation(sgw[:, 0:now], wld[:, 0:now], AF.Sign))
                act(a1)
                for ot in range(nol):
                    ow = min(512, now - 512 * ot)
                    def mm(t, ot=ot, ow=ow):
                        seq(t.matmul(pS4[:, 0:ow], onescf[:, :], absw[:, 512 * ot:512 * ot + ow], start=True, stop=True))
                    ten(mm)
                    def v1(v, ot=ot, ow=ow):
                        seq(v.tensor_scalar(screw[:, 512 * ot:512 * ot + ow], pS4[:, 0:ow], 1.0 / 256.0, 0.5e-5,
                                            ALU.mult, ALU.add))
                    vec(v1)
                    def mm2(t, ot=ot, ow=ow):
                        seq(t.matmul(pB4[:, 0:ow], onesr[:, :], screw[:, 512 * ot:512 * ot + ow], start=True, stop=True))
                    ten(mm2)
                    def v2(v, ot=ot, ow=ow):
                        sl = slice(512 * ot, 512 * ot + ow)
                        seq(v.tensor_tensor(mbw[:, 0:ow], absw[:, sl], pB4[:, 0:ow], ALU.is_ge))
                        v.drain()
                        seq(v.tensor_tensor(tbw[:, 0:ow], mbw[:, 0:ow], sgw[:, sl], ALU.mult))
                        v.drain()
                        seq(v.tensor_tensor(tbw[:, 0:ow], tbw[:, 0:ow], pB4[:, 0:ow], ALU.mult))
                        v.drain()
                        seq(v.tensor_scalar(wqb[:, sl], tbw[:, 0:ow], 2.0, None, ALU.mult))
                    vec(v2)
                def dw(s, Wdst=Wdst, ic=ic, now=now):
                    dma(s, Wdst[128 * ic:128 * (ic + 1), :], wqb[:, 0:now])
                dms(dw)

    with nc.sbuf_tensor("xqs", [128, 16 * T], BF16) as xqs, \
         nc.sbuf_tensor("wsg", [128, 16 * 512], BF16) as wsg, \
         nc.sbuf_tensor("wsu", [128, 16 * 512], BF16) as wsu, \
         nc.sbuf_tensor("gsc2", [128, 512], F32) as gsc2, \
         nc.sbuf_tensor("usc2", [128, 512], F32) as usc2, \
         nc.sbuf_tensor("hstr", [128, 512], F32) as hstr, \
         nc.sbuf_tensor("hbb", [128, 512], BF16) as hbb, \
         nc.psum_tensor("pA5", [128, 512], F32) as pA5, \
         nc.psum_tensor("pB5", [128, 512], F32) as pB5:
        def dl0(s):
            for ic in range(16):
                dma(s, xqs[:, T * ic:T * (ic + 1)], xqsp[128 * ic:128 * (ic + 1), :])
        dms(dl0)
        for so in range(11):
            ow = min(512, F1 - 512 * so)
            def wl(s, so=so, ow=ow):
                for ic in range(16):
                    dma(s, wsg[:, 512 * ic:512 * ic + ow], wq1g[128 * ic:128 * (ic + 1), 512 * so:512 * so + ow])
                    dma(s, wsu[:, 512 * ic:512 * ic + ow], wq1u[128 * ic:128 * (ic + 1), 512 * so:512 * so + ow])
            dms(wl)
            for tc in range(NTC):
                def mm(t, tc=tc, ow=ow):
                    for ic in range(16):
                        lhs = xqs[:, T * ic + 128 * tc:T * ic + 128 * (tc + 1)]
                        seq(t.matmul(pA5[:, 0:ow], lhs, wsg[:, 512 * ic:512 * ic + ow], start=(ic == 0), stop=(ic == 15)))
                        seq(t.matmul(pB5[:, 0:ow], lhs, wsu[:, 512 * ic:512 * ic + ow], start=(ic == 0), stop=(ic == 15)))
                ten(mm)
                def v1(v, tc=tc, ow=ow):
                    seq(v.tensor_scalar(gsc2[:, 0:ow], pA5[:, 0:ow], amaxs[:, tc:tc + 1], None, ALU.mult))
                    seq(v.tensor_scalar(usc2[:, 0:ow], pB5[:, 0:ow], amaxs[:, tc:tc + 1], None, ALU.mult))
                vec(v1)
                def a1(a, ow=ow):
                    seq(a.activation(gsc2[:, 0:ow], gsc2[:, 0:ow], AF.Silu))
                act(a1)
                def v2(v, tc=tc, so=so, ow=ow):
                    seq(v.tensor_tensor(hstr[:, 0:ow], gsc2[:, 0:ow], usc2[:, 0:ow], ALU.mult))
                    v.drain()
                    seq(v.tensor_copy(hbb[:, 0:ow], hstr[:, 0:ow]))
                    seq(v.tensor_reduce(hmaxes[:, 11 * tc + so:11 * tc + so + 1], hstr[:, 0:ow], AX.X, ALU.max,
                                        apply_absolute_value=True))
                vec(v2)
                def dw(s, tc=tc, so=so, ow=ow):
                    dma(s, h1sp[128 * tc:128 * (tc + 1), 512 * so:512 * so + ow], hbb[:, 0:ow])
                dms(dw)

    with nc.sbuf_tensor("wdh", [128, 43 * 1024], BF16) as wdh, \
         nc.sbuf_tensor("hrow", [128, F1], BF16) as hrow, \
         nc.sbuf_tensor("hqf", [128, F1], F32) as hqf, \
         nc.sbuf_tensor("hmf", [128, 512], F32) as hmf, \
         nc.sbuf_tensor("hqfm", [128, F1], BF16) as hqfm, \
         nc.sbuf_tensor("hm2", [128, 1], F32) as hm2, \
         nc.sbuf_tensor("hs2", [128, 1], F32) as hs2, \
         nc.sbuf_tensor("hss", [128, 1], F32) as hss, \
         nc.sbuf_tensor("x1l", [128, 1024], F32) as x1l, \
         nc.sbuf_tensor("s1l", [128, 1024], BF16) as s1l, \
         nc.sbuf_tensor("s1f", [128, 1024], F32) as s1f, \
         nc.sbuf_tensor("gfin", [128, 1024], F32) as gfin, \
         nc.sbuf_tensor("oby", [128, 1024], F32) as oby, \
         nc.psum_tensor("pD5", [128, 1024], F32) as pD5, \
         nc.psum_tensor("pT5", [128, 512], F32) as pT5:
        for dh in range(2):
            def wl(s, dh=dh):
                for c in range(43):
                    dma(s, wdh[:, 1024 * c:1024 * (c + 1)], wq1d[128 * c:128 * (c + 1), 1024 * dh:1024 * (dh + 1)])
            dms(wl)
            for tc in range(NTC):
                def dl(s, tc=tc, dh=dh):
                    dma(s, hrow[:, :], h1sp[128 * tc:128 * (tc + 1), :])
                    dma(s, x1l[:, :], x1sp[128 * tc:128 * (tc + 1), 1024 * dh:1024 * (dh + 1)])
                    dma(s, s1l[:, :], s1sp[128 * tc:128 * (tc + 1), 1024 * dh:1024 * (dh + 1)])
                dms(dl)
                def v1(v, tc=tc):
                    seq(v.tensor_reduce(hm2[:, :], hmaxes[:, 11 * tc:11 * (tc + 1)], AX.X, ALU.max))
                    v.drain()
                    seq(v.tensor_scalar(hm2[:, :], hm2[:, :], 1e-5, None, ALU.max))
                vec(v1)
                def v2(v):
                    seq(v.reciprocal(hs2[:, :], hm2[:, :]))
                    v.drain()
                    seq(v.tensor_scalar(hs2[:, :], hs2[:, :], 127.0, None, ALU.mult))
                    seq(v.tensor_scalar(hss[:, :], hm2[:, :], 1.0 / 127.0, None, ALU.mult))
                    v.drain()
                    seq(v.tensor_scalar(hqf[:, :], hrow[:, :], hs2[:, :], None, ALU.mult))
                    v.drain()
                    seq(v.tensor_scalar(hqf[:, :], hqf[:, :], MAGIC, MAGIC, ALU.add, ALU.subtract))
                vec(v2)
                for rr in range(11):
                    def tr(t, rr=rr):
                        for j in range(4):
                            i = 4 * rr + j
                            if i < 43:
                                seq(t.transpose(pT5[:, 128 * j:128 * (j + 1)], hqf[:, 128 * i:128 * (i + 1)], i128[:, :]))
                    ten(tr)
                    def ve(v, rr=rr):
                        for j in range(4):
                            i = 4 * rr + j
                            if i < 43:
                                seq(v.tensor_copy(hqfm[:, 128 * i:128 * (i + 1)], pT5[:, 128 * j:128 * (j + 1)]))
                    vec(ve)
                def mm(t):
                    for dj in range(2):
                        for c in range(43):
                            seq(t.matmul(pD5[:, 512 * dj:512 * (dj + 1)], hqfm[:, 128 * c:128 * (c + 1)],
                                         wdh[:, 1024 * c + 512 * dj:1024 * c + 512 * (dj + 1)],
                                         start=(c == 0), stop=(c == 42)))
                ten(mm)
                def v3(v):
                    seq(v.tensor_scalar(gfin[:, 0:512], pD5[:, 0:512], hss[:, :], None, ALU.mult))
                    seq(v.tensor_scalar(gfin[:, 512:1024], pD5[:, 512:1024], hss[:, :], None, ALU.mult))
                    seq(v.tensor_copy(s1f[:, :], s1l[:, :]))
                vec(v3)
                def a2(a):
                    seq(a.activation(gfin[:, :], gfin[:, :], AF.Silu))
                act(a2)
                def v4b(v):
                    seq(v.tensor_tensor(gfin[:, :], gfin[:, :], s1f[:, :], ALU.mult))
                    v.drain()
                    seq(v.tensor_tensor(oby[:, :], gfin[:, :], x1l[:, :], ALU.add))
                vec(v4b)
                def dw(s, tc=tc, dh=dh):
                    dma(s, OUT[128 * tc:128 * (tc + 1), 1024 * dh:1024 * (dh + 1)], oby[:, :])
                dms(dw)

    def fin(s):
        s.nop()
    dms(fin)
    st.close()
    return nc


def kernel(L0_attn_x, L0_attn_W_in, L0_attn_conv_w, L0_attn_conv_b, L0_attn_dt_bias,
           L0_attn_A_log, L0_attn_D, L0_attn_norm_w, L0_attn_W_out,
           L0_ffn_w_gate, L0_ffn_w_up, L0_ffn_w_down,
           L1_attn_wq, L1_attn_wk, L1_attn_wv, L1_attn_wp, L1_attn_qg,
           L1_ffn_w_gate, L1_ffn_w_up, L1_ffn_w_down):
    f = np.float32
    x = np.ascontiguousarray(np.asarray(L0_attn_x, f))
    WinT = np.ascontiguousarray(np.asarray(L0_attn_W_in, f).T)
    cwt = np.asarray(L0_attn_conv_w, f).reshape(33, 128, 4).transpose(1, 0, 2).reshape(128, 132)
    cbt = np.asarray(L0_attn_conv_b, f).reshape(33, 128).T
    dtb = np.asarray(L0_attn_dt_bias, f).reshape(64, 1)
    alog = np.asarray(L0_attn_A_log, f).reshape(1, 64)
    drow = np.asarray(L0_attn_D, f).reshape(1, 64)
    nw = np.asarray(L0_attn_norm_w, f).reshape(1, DIN)
    WoutT = np.ascontiguousarray(np.asarray(L0_attn_W_out, f).T)
    perm = np.concatenate([np.arange(m, I0, 4) for m in range(4)])
    WG0 = np.ascontiguousarray(np.asarray(L0_ffn_w_gate, f)[:, perm])
    WU0 = np.ascontiguousarray(np.asarray(L0_ffn_w_up, f)[:, perm])
    WD0 = np.ascontiguousarray(np.asarray(L0_ffn_w_down, f)[perm, :])
    WQT = np.ascontiguousarray(np.asarray(L1_attn_wq, f).T)
    WKT = np.ascontiguousarray(np.asarray(L1_attn_wk, f).T)
    WVT = np.ascontiguousarray(np.asarray(L1_attn_wv, f).T)
    WPT = np.ascontiguousarray(np.asarray(L1_attn_wp, f).T)
    qg = np.asarray(L1_attn_qg, f).reshape(1, 16)
    WG1 = np.ascontiguousarray(np.asarray(L1_ffn_w_gate, f).T)
    WU1 = np.ascontiguousarray(np.asarray(L1_ffn_w_up, f).T)
    WD1 = np.ascontiguousarray(np.asarray(L1_ffn_w_down, f).T)
    I128 = np.eye(128, dtype=f)
    kk = np.arange(128)
    TRI = (kk[:, None] <= kk[None, :]).astype(f)
    TRIB = TRI.astype(ml_dtypes.bfloat16)
    qq = np.arange(512)
    DMSK = np.stack([((128 * r + kk[:, None]) <= qq[None, :]) for r in range(4)]).astype(ml_dtypes.bfloat16)
    t_ = np.arange(T, dtype=np.float64)
    inv = 1.0 / (10000.0 ** (np.arange(0, 16, 2, dtype=np.float64) / 16.0))
    fr = t_[:, None] * inv[None, :]
    COST = np.cos(fr).astype(f)
    SINT = np.sin(fr).astype(f)

    nc = build()
    nwc_h = np.ascontiguousarray(nw.reshape(32, 128).T)
    base = dict(WinT=WinT, CW=cwt, CB=cbt, DTB=dtb, ALOG=alog, DROW=drow, NW=nw, NWC=nwc_h,
                WoutT=WoutT, WG0=WG0, WU0=WU0, WD0=WD0, WQT=WQT, WKT=WKT, WVT=WVT,
                WPT=WPT, QG=qg, WG1=WG1, WU1=WU1, WD1=WD1, I128=I128, TRI=TRI,
                TRIB=TRIB, DMSK=DMSK, COST=COST, SINT=SINT)
    in_maps = []
    for c in range(8):
        m = dict(base)
        m["X"] = x[c % 4]
        in_maps.append(m)
    res = run_bass_kernel_spmd(nc, in_maps, core_ids=list(range(8)))
    out = np.stack([np.asarray(res.results[b]["OUT"], f) for b in range(4)])
    return out


